# version 05
# baseline (speedup 1.0000x reference)
import math
from contextlib import ExitStack

import numpy as np
import ml_dtypes
import concourse.bass as bass
import concourse.mybir as mybir
from concourse.bass_utils import run_bass_kernel_spmd

F32 = mybir.dt.float32
BF16 = mybir.dt.bfloat16
AF = mybir.ActivationFunctionType
ALU = mybir.AluOpType
AX = mybir.AxisListType

P = 128
S = 4096
H = 2048
NH = 16
HD = 128
NI = 8192
NC = H // P
NG = NI // P
TOK = 2048
NT = TOK // P
LKP = [4096, 3072, 2048, 1024]
NKT = [lk // P for lk in LKP]
WIN = 8
BLOCKS_LO = [7, 4, 3, 0]
BLOCKS_HI = [6, 5, 2, 1]
SCALE = 1.0 / math.sqrt(HD)
SHIFT = -30.0
EPS = 1e-5


class Ctr:
    def __init__(self, stack, nc, name):
        self.sem = stack.enter_context(nc.semaphore(name))
        self.n = 0

    def inc(self, instr, k=1):
        instr.then_inc(self.sem, k)
        self.n += k
        return instr

    def mark(self):
        return (self.sem, self.n)


def wait(eng, *marks):
    for m in marks:
        if m is None:
            continue
        sem, n = m
        if n > 0:
            eng.wait_ge(sem, n)


class DmaQ:

    def __init__(self, stack, nc, name, eng, nslots):
        self.eng = eng
        self.ctrs = [Ctr(stack, nc, f"{name}{i}") for i in range(nslots)]

    def start(self, slot, out, in_, pre=(), transpose=False, **kw):
        c = self.ctrs[slot]
        wait(self.eng, *pre)
        if transpose:
            c.inc(self.eng.dma_start_transpose(out, in_), 16)
        else:
            c.inc(self.eng.dma_start(out, in_, **kw), 16)
        return c.mark()


def build_nc(debug=()):
    nc = bass.Bass("TRN2", target_bir_lowering=False, detect_race_conditions=False)
    st = ExitStack()

    def dram(name, shape, dtype, kind="ExternalInput"):
        return nc.dram_tensor(name, shape, dtype, kind=kind).ap()

    x_own = dram("x_own", [TOK, H], F32)
    x_kv = dram("x_kv", [S, H], F32)
    wq_r = dram("wq_r", [P, NC, H], F32)
    wk_r = dram("wk_r", [P, NC, H], F32)
    wv_r = dram("wv_r", [P, NC, H], F32)
    wo_r = dram("wo_r", [P, NH, H], F32)
    wgu_t = dram("wgu_t", [NG * 2, P, NC, P], F32)
    wd_t = dram("wd_t", [NC, P, NG, P], F32)
    cos_own = dram("cos_own", [P, NT, 64], F32)
    sin_own = dram("sin_own", [P, NT, 64], F32)
    cos_kv = dram("cos_kv", [P, S // P, 64], F32)
    sin_kv = dram("sin_kv", [P, S // P, 64], F32)
    maskT = dram("maskT", [P, 4, WIN, 512], BF16)
    ident = dram("ident", [P, P], F32)
    ones_bf = dram("ones_bf", [P, P], BF16)
    ones_f32 = dram("ones_f32", [P, 1], F32)

    out = dram("out", [TOK, H], F32, kind="ExternalOutput")

    def scr(name, shape, dtype):
        kind = "ExternalOutput" if name in debug else "Internal"
        return dram(name, shape, dtype, kind=kind)

    xbf_kv = scr("xbf_kv", [S, H], BF16)
    xbf_own = scr("xbf_own", [TOK, H], BF16)
    k_sp = scr("k_sp", [S, H], BF16)
    v_sp = scr("v_sp", [S, H], BF16)
    q_sp = scr("q_sp", [TOK, H], BF16)
    y_sp = scr("y_sp", [NC, P, TOK], F32)
    h_sp = scr("h_sp", [NG, P, TOK], BF16)
    z_sp = scr("z_sp", [TOK, H], F32)
    ctx_dump = scr("ctx_dump", [P, NH * TOK], BF16) if "ctx_dump" in debug else None
    st1_dump = scr("st1_dump", [NT, P, 8], F32) if "p3z" in debug else None

    T, V, Sc, G, Y = nc.tensor, nc.vector, nc.scalar, nc.gpsimd, nc.sync
    tc = Ctr(st, nc, "tC")
    vc = Ctr(st, nc, "vC")
    sc = Ctr(st, nc, "sC")
    setup = Ctr(st, nc, "setup")
    setup_g = Ctr(st, nc, "setupG")
    qy = DmaQ(st, nc, "qy", Y, 8)
    qg = DmaQ(st, nc, "qg", G, 8)
    qs = DmaQ(st, nc, "qs", Sc, 8)

    def sb(name, shape, dtype):
        return st.enter_context(nc.sbuf_tensor(name, shape, dtype))

    ident_sb = sb("ident_sb", [P, P], F32)
    ones_sb = sb("ones_sb", [P, P], BF16)
    ones1_sb = sb("ones1_sb", [P, 1], F32)
    consts = sb("consts", [P, 2], F32)
    setup.inc(Y.dma_start(ident_sb[:], ident[:]), 16)
    setup.inc(Y.dma_start(ones_sb[:], ones_bf[:]), 16)
    setup.inc(Y.dma_start(ones1_sb[:], ones_f32[:]), 16)
    V.memset(consts[:, 0:1], SCALE)
    vc.inc(V.memset(consts[:, 1:2], SHIFT))
    m_consts = vc.mark()
    c_scale, c_shift = consts[:, 0:1], consts[:, 1:2]

    p1 = ExitStack()
    w_a = p1.enter_context(nc.sbuf_tensor("w_a", [P, NC, H], BF16))
    w_b = p1.enter_context(nc.sbuf_tensor("w_b", [P, NC, H], BF16))
    tab = p1.enter_context(nc.sbuf_tensor("tab", [P, 2, 48, 64], F32))
    xfm = [p1.enter_context(nc.sbuf_tensor(f"xfm{i}", [P, NC, 512], BF16))
           for i in range(2)]
    rsc = p1.enter_context(nc.sbuf_tensor("rsc", [P, 4, NH // 2, 64], F32))
    cast_ctrs = {}
    for nm, nst in (("own", TOK // 512), ("kv", S // 512)):
        for j in range(nst):
            cast_ctrs[(nm, j)] = Ctr(st, nc, f"xc_{nm}{j}")

    def issue_cast(nm, j):
        srcap = x_own if nm == "own" else x_kv
        dstap = xbf_own if nm == "own" else xbf_kv
        cast_ctrs[(nm, j)].inc(
            G.dma_start(dstap[j * 512:(j + 1) * 512, :],
                        srcap[j * 512:(j + 1) * 512, :]), 16)

    po = [p1.enter_context(nc.sbuf_tensor(f"po{i}", [P, H], BF16)) for i in range(2)]
    mm_h = [p1.enter_context(nc.psum_tensor(f"mm_h{i}", [P, H // 2], F32))
            for i in range(2)]

    setup.inc(Y.dma_start(tab[:, 0, 0:16, :], cos_own[:]), 16)
    setup.inc(Y.dma_start(tab[:, 1, 0:16, :], sin_own[:]), 16)
    setup.inc(Y.dma_start(tab[:, 0, 16:48, :], cos_kv[:]), 16)
    setup.inc(Y.dma_start(tab[:, 1, 16:48, :], sin_kv[:]), 16)
    m_setup = setup.mark()

    scope_p1 = nc.enter_named_scope("p1", False)
    state = dict(prev_evac=None, prev_rope=None, obuf=0, spill={},
                 hfree={}, tload={}, mmdone={})

    issue_cast("own", 0)
    m_wq = qg.start(0, w_a[:].rearrange("p c o -> p (c o)"),
                    wq_r[:].rearrange("p c o -> p (c o)"))
    for j in range(1, TOK // 512):
        issue_cast("own", j)
    m_wv = qg.start(1, w_b[:].rearrange("p c o -> p (c o)"),
                    wv_r[:].rearrange("p c o -> p (c o)"))
    for j in range(S // 512):
        issue_cast("kv", j)

    def p1_stripe(si, nm, stripe, tabbase, wlist):
        b = si % 2
        xbf = xbf_own if nm == "own" else xbf_kv
        pre = [cast_ctrs[(nm, stripe)].mark(), state["mmdone"].get(si - 2)]
        for c in range(NC):
            mT = qy.start(b, xfm[b][:, c, :],
                          xbf[stripe * 512:(stripe + 1) * 512,
                              c * P:(c + 1) * P],
                          pre=(pre if c == 0 else ()), transpose=True)
        state["tload"][si] = mT
        for j in range(4):
            tile = stripe * 4 + j
            for (w_sb, m_w, do_rope, sp, sp_row) in wlist:
                qmarks = m_w if isinstance(m_w, list) else None
                ob = state["obuf"] % 2
                state["obuf"] += 1
                for half in range(2):
                    hm = half * (NH // 2)
                    wait(T, mT, state["hfree"].get(half))
                    if qmarks is None:
                        wait(T, m_w)
                    for n in (0, 1):
                        nn = 2 * half + n
                        for c in range(NC):
                            if qmarks is not None and c % 4 == 0:
                                wait(T, qmarks[c // 4])
                            ins = T.matmul(mm_h[half][:, n * 512:(n + 1) * 512],
                                           xfm[b][:, c, j * P:(j + 1) * P],
                                           w_sb[:, c, nn * 512:(nn + 1) * 512],
                                           start=(c == 0), stop=(c == NC - 1))
                    tc.inc(ins)
                    m_mm = tc.mark()
                    wait(V, m_mm, m_consts)
                    if half == 0:
                        wait(V, state["spill"].get(ob))
                    pv = mm_h[half][:].rearrange("p (h j) -> p h j", h=NH // 2)
                    ov = po[ob][:].rearrange("p (h j) -> p h j", h=NH)[:,
                                                                      hm:hm + NH // 2]
                    if do_rope:
                        cosb = tab[:, 0, tabbase + tile, :].unsqueeze(1) \
                            .broadcast_to([P, NH // 2, 64])
                        sinb = tab[:, 1, tabbase + tile, :].unsqueeze(1) \
                            .broadcast_to([P, NH // 2, 64])
                        rv = rsc[:]
                        V.tensor_tensor(rv[:, 0], pv[:, :, 0:64], cosb, op=ALU.mult)
                        V.tensor_tensor(rv[:, 1], pv[:, :, 64:128], sinb, op=ALU.mult)
                        V.tensor_tensor(rv[:, 2], pv[:, :, 64:128], cosb, op=ALU.mult)
                        V.tensor_tensor(rv[:, 3], pv[:, :, 0:64], sinb, op=ALU.mult)
                        V.tensor_tensor(ov[:, :, 0:64], rv[:, 0], rv[:, 1],
                                        op=ALU.subtract)
                        vc.inc(V.tensor_tensor(ov[:, :, 64:128], rv[:, 2], rv[:, 3],
                                               op=ALU.add))
                    else:
                        wait(Sc, m_mm)
                        if half == 0:
                            wait(Sc, state["spill"].get(ob))
                        sc.inc(Sc.copy(
                            po[ob][:, half * (H // 2):(half + 1) * (H // 2)],
                            mm_h[half][:]))
                    state["hfree"][half] = (sc.mark() if not do_rope else vc.mark())
                state["prev_rope"] = vc.mark() if do_rope else sc.mark()
                state["spill"][ob] = qs.start(state["obuf"] % 4,
                                              sp[tile * P:(tile + 1) * P, :],
                                              po[ob][:], pre=[state["prev_rope"]])
        state["mmdone"][si] = tc.mark()

    for si in range(TOK // 512):
        p1_stripe(si, "own", si, 0, [(w_a, m_wq, True, q_sp, None)])
    m_q_done = tc.mark()
    wk_q = []
    for qtr in range(4):
        wk_q.append(qg.start([2, 3, 6, 7][qtr],
                             w_a[:, 4 * qtr:4 * qtr + 4, :]
                             .rearrange("p c o -> p (c o)"),
                             wk_r[:, 4 * qtr:4 * qtr + 4, :]
                             .rearrange("p c o -> p (c o)"),
                             pre=([m_q_done] if qtr == 0 else [])))
    m_wk = wk_q[3]
    for sj in range(S // 512):
        mk_w = wk_q if sj == 0 else m_wk
        p1_stripe(TOK // 512 + sj, "kv", sj, 16,
                  [(w_a, mk_w, True, k_sp, None), (w_b, m_wv, False, v_sp, None)])

    m_p1_t, m_p1_v = tc.mark(), vc.mark()
    m_p1_sp = [c.mark() for c in qs.ctrs]
    nc.leave_named_scope("p1", scope_p1[0], False)
    p1.close()

    if "p1" in debug:
        wait(Y, m_p1_t, m_p1_v, *m_p1_sp)
        st.close()
        return nc

    ctxstk = ExitStack()
    ctx_res = ctxstk.enter_context(nc.sbuf_tensor("ctx_res", [P, NH, TOK], BF16))
    p2 = ExitStack()
    mask_sb = p2.enter_context(nc.sbuf_tensor("mask_sb", [P, 4 * WIN * 512], BF16))
    kfm = [p2.enter_context(nc.sbuf_tensor(f"kfm{i}", [P, S], BF16)) for i in range(2)]
    qfm = [p2.enter_context(nc.sbuf_tensor(f"qfm{i}", [P, TOK], BF16)) for i in range(2)]
    vtm = [p2.enter_context(nc.sbuf_tensor(f"vtm{i}", [P, S // P, HD], BF16))
           for i in range(2)]
    probs = [p2.enter_context(nc.sbuf_tensor(f"probs{i}", [P, 512], BF16))
             for i in range(4)]
    recip = [p2.enter_context(nc.sbuf_tensor(f"recip{i}", [P, 512], F32))
             for i in range(2)]
    sc_ps = [p2.enter_context(nc.psum_tensor(f"sc_ps{i}", [P, 512], F32))
             for i in range(3)]
    ctx_ps = [p2.enter_context(nc.psum_tensor(f"ctx_ps{i}", [P, 512], F32))
              for i in range(2)]
    sum_ps = [p2.enter_context(nc.psum_tensor(f"sum_ps{i}", [P, 512], F32))
              for i in range(2)]

    mask_ctr = Ctr(st, nc, "maskL")
    wait(G, m_p1_t, m_p1_v, *m_p1_sp)
    mask_ctr.inc(G.dma_start(mask_sb[:],
                             maskT[:].rearrange("p a b c -> p (a b c)")), 16)
    m_mask = mask_ctr.mark()

    scope_p2 = nc.enter_named_scope("p2", False)
    v_view = v_sp[:].rearrange("(t p) c -> p t c", p=P)
    head_marks = {}
    head_done = {}

    def load_head(h):
        pre = [m_p1_sp[0], m_p1_sp[1], m_p1_sp[2], m_p1_sp[3]] if h < 2 else \
            [head_done[h - 2]]
        mk = qy.start(2 + 2 * (h % 2), kfm[h % 2][:],
                      k_sp[:, h * HD:(h + 1) * HD], pre=pre, transpose=True)
        mq = qy.start(3 + 2 * (h % 2), qfm[h % 2][:],
                      q_sp[:, h * HD:(h + 1) * HD], transpose=True)
        mv = qg.start(4 + (h % 2), vtm[h % 2][:],
                      v_view[:, :, h * HD:(h + 1) * HD],
                      pre=pre, allow_slow_non_contiguous=True)
        head_marks[h] = (mk, mq, mv)

    load_head(0)
    exp_m, pv_m, evac_m = {}, {}, {}
    items = []
    for h in range(NH):
        for s in range(4):
            for kt in range(NKT[s]):
                items.append((len(items), h, s, kt))

    def emit_qk(k, h, s, kt):
        wait(T, exp_m.get(k - 3))
        tc.inc(T.matmul(sc_ps[k % 3][:],
                        kfm[h % 2][:, kt * P:(kt + 1) * P],
                        qfm[h % 2][:, s * 512:(s + 1) * 512],
                        start=True, stop=True))
        m_qk = tc.mark()
        wait(Sc, pv_m.get(k - 4), m_qk)
        sc.inc(Sc.activation(probs[k % 4][:], sc_ps[k % 3][:], AF.Exp,
                             bias=c_shift, scale=c_scale))
        exp_m[k] = sc.mark()
        dep = exp_m[k]
        if kt >= NKT[s] - WIN:
            w = kt - (NKT[s] - WIN)
            wait(V, exp_m[k], m_mask)
            msl = mask_sb[:, (s * WIN + w) * 512:(s * WIN + w + 1) * 512]
            vc.inc(V.tensor_tensor(probs[k % 4][:], probs[k % 4][:], msl,
                                   op=ALU.mult))
            dep = vc.mark()
        return dep

    def emit_pv(k, h, s, kt, dep):
        sl = 4 * h + s
        first, last = kt == 0, kt == NKT[s] - 1
        wait(T, dep)
        if first:
            wait(T, evac_m.get(sl - 2))
        T.matmul(ctx_ps[sl % 2][:], vtm[h % 2][:, kt, :], probs[k % 4][:],
                 start=first, stop=last)
        tc.inc(T.matmul(sum_ps[sl % 2][:], ones_sb[:], probs[k % 4][:],
                        start=first, stop=last))
        pv_m[k] = tc.mark()
        if last:
            wait(V, pv_m[k])
            V.reciprocal(recip[sl % 2][:], sum_ps[sl % 2][:])
            vc.inc(V.tensor_tensor(ctx_res[:, h, s * 512:(s + 1) * 512],
                                   ctx_ps[sl % 2][:], recip[sl % 2][:],
                                   op=ALU.mult))
            evac_m[sl] = vc.mark()
            if s == 3:
                head_done[h] = tc.mark()

    from collections import deque
    pend = deque()
    for (k, h, s, kt) in items:
        if s == 0 and kt == 0:
            while pend:
                emit_pv(*pend.popleft())
            if h + 1 < NH:
                load_head(h + 1)
            wait(T, *head_marks[h], m_setup)
        dep = emit_qk(k, h, s, kt)
        pend.append((k, h, s, kt, dep))
        if len(pend) > 2:
            emit_pv(*pend.popleft())
    while pend:
        emit_pv(*pend.popleft())
    m_p2_t, m_p2_v = tc.mark(), vc.mark()
    nc.leave_named_scope("p2", scope_p2[0], False)
    p2.close()

    if "p2" in debug:
        wait(Y, m_p2_v)
        fin = qy.start(0, ctx_dump[:],
                       ctx_res[:].rearrange("p h t -> p (h t)"), pre=[m_p2_v])
        wait(Y, fin)
        ctxstk.close()
        st.close()
        return nc

    p3 = ExitStack()
    wo_sb = p3.enter_context(nc.sbuf_tensor("wo_sb", [P, NH, H], BF16))
    x3 = [p3.enter_context(nc.sbuf_tensor(f"x3{i}", [P, H], F32)) for i in range(2)]
    z1 = p3.enter_context(nc.sbuf_tensor("z1", [P, H], F32))
    yt = [p3.enter_context(nc.sbuf_tensor(f"yt{i}", [P, H], F32))
          for i in range(2)]
    yspb = [p3.enter_context(nc.sbuf_tensor("yspb0", [P, H], F32))]
    st1 = p3.enter_context(nc.sbuf_tensor("st1", [P, 8], F32))
    o_h = [p3.enter_context(nc.psum_tensor(f"o_h{i}", [P, H // 2], F32))
           for i in range(2)]
    tp3_ps = p3.enter_context(nc.psum_tensor("tp3_ps", [P, H], F32))

    scope_p3 = nc.enter_named_scope("p3", False)
    wo_ctr = Ctr(st, nc, "woL")
    wait(G, m_p2_t, m_p2_v)
    wo_q = []
    for qtr in range(4):
        wo_q.append(qg.start([2, 3, 6, 7][qtr],
                             wo_sb[:, 4 * qtr:4 * qtr + 4, :]
                             .rearrange("p h o -> p (h o)"),
                             wo_r[:, 4 * qtr:4 * qtr + 4, :]
                             .rearrange("p h o -> p (h o)")))
    m_wo = wo_q[3]

    zadd3, ysp3, tpev3 = {}, {}, None
    addh3, app3 = {}, {}
    pend3 = None

    def emit_o(t):
        m_x = qy.start(t % 2, x3[t % 2][:], x_own[t * P:(t + 1) * P, :],
                       pre=([m_p2_t, m_p2_v] if t < 2 else [app3[t - 2]]))
        for half in range(2):
            wait(T, m_p2_v if t == 0 else None, addh3.get((t - 1, half)))
            for n in (2 * half, 2 * half + 1):
                for h in range(NH):
                    if t < 2 and h % 4 == 0:
                        wait(T, wo_q[h // 4])
                    ins = T.matmul(o_h[half][:, (n - 2 * half) * 512:
                                              (n - 2 * half + 1) * 512],
                                   ctx_res[:, h, t * P:(t + 1) * P],
                                   wo_sb[:, h, n * 512:(n + 1) * 512],
                                   start=(h == 0), stop=(h == NH - 1))
            tc.inc(ins)
            m_omm = tc.mark()
            wait(V, m_omm, m_x, app3.get(t - 1) if half == 0 else None)
            vc.inc(V.tensor_tensor(z1[:, half * 1024:(half + 1) * 1024],
                                   o_h[half][:],
                                   x3[t % 2][:, half * 1024:(half + 1) * 1024],
                                   op=ALU.add))
            addh3[(t, half)] = vc.mark()
        zadd3[t] = addh3[(t, 1)]
        vc.inc(V.reduce_sum(st1[:, 0:1], z1[:], axis=AX.X))
        m_z1 = vc.mark()
        wait(Sc, m_z1, emit_o.tpmark.get(t - 2))
        sc.inc(Sc.activation(yt[t % 2][:], z1[:], AF.Square, accum_out=st1[:, 1:2]))
        m_sq = sc.mark()
        wait(V, m_sq)
        V.tensor_scalar_mul(st1[:, 2:3], st1[:, 0:1], 1.0 / H)
        vc.inc(V.tensor_scalar_mul(st1[:, 3:4], st1[:, 1:2], 1.0 / H))
        wait(V, vc.mark())
        vc.inc(V.tensor_tensor(st1[:, 4:5], st1[:, 2:3], st1[:, 2:3], op=ALU.mult))
        wait(V, vc.mark())
        vc.inc(V.tensor_tensor(st1[:, 4:5], st1[:, 3:4], st1[:, 4:5],
                               op=ALU.subtract))
        wait(V, vc.mark())
        vc.inc(V.tensor_scalar_add(st1[:, 4:5], st1[:, 4:5], EPS))
        wait(V, vc.mark())
        vc.inc(V.reciprocal(st1[:, 5:6], st1[:, 4:5]))
        m_v1 = vc.mark()
        wait(Sc, m_v1)
        sc.inc(Sc.activation(st1[:, 6:7], st1[:, 5:6], AF.Sqrt))
        m_rstd = sc.mark()
        wait(V, m_rstd)
        vc.inc(V.tensor_scalar(yt[t % 2][:], z1[:], st1[:, 2:3], st1[:, 6:7],
                               op0=ALU.subtract, op1=ALU.mult))
        app3[t] = vc.mark()
        return app3[t]

    def emit_ytp(t, m_y):
        wait(T, m_y, emit_ytp.tpev)
        for c in range(NC):
            ins = T.transpose(tp3_ps[:, c * P:(c + 1) * P],
                              yt[t % 2][:, c * P:(c + 1) * P], ident_sb[:])
        tc.inc(ins)
        m_tp = tc.mark()
        emit_o.tpmark[t] = m_tp
        wait(V, m_tp, ysp3.get(t - 1))
        vc.inc(V.tensor_copy(yspb[0][:], tp3_ps[:]))
        m_ye = vc.mark()
        emit_ytp.tpev = m_ye
        ysp3[t] = qy.start(4 + t % 2,
                           y_sp[:, :, t * P:(t + 1) * P].rearrange("c p j -> p c j"),
                           yspb[0][:].rearrange("p (c j) -> p c j", c=NC),
                           pre=[m_ye], allow_slow_non_contiguous=True)

    emit_ytp.tpev = None
    emit_o.tpmark = {}
    for t in range(NT):
        m_y = emit_o(t)
        if pend3 is not None:
            emit_ytp(*pend3)
        pend3 = (t, m_y)
    emit_ytp(*pend3)
    m_p3_t, m_p3_v, m_p3_s = tc.mark(), vc.mark(), sc.mark()
    m_p3_ysp = [ysp3[t] for t in (NT - 2, NT - 1)]
    nc.leave_named_scope("p3", scope_p3[0], False)
    p3.close()
    ctxstk.close()

    if "p3" in debug:
        wait(Y, m_p3_t, m_p3_v, m_p3_s, *m_p3_ysp)
        st.close()
        return nc

    hstk = ExitStack()
    hresA = hstk.enter_context(nc.sbuf_tensor("hresA", [P, NG // 2, TOK // 2], BF16,
                                              side="right"))
    wd5 = [hstk.enter_context(nc.sbuf_tensor(f"wd5{i}", [P, NG, P], BF16,
                                             side="right")) for i in range(2)]
    scope_p4 = nc.enter_named_scope("p4", False)
    p4 = ExitStack()
    yfm = p4.enter_context(nc.sbuf_tensor("yfm", [P, NC, TOK], BF16))
    yfm_q = []
    wait(G, m_p3_t, m_p3_v, m_p3_s, *m_p3_ysp)
    for qtr in range(4):
        yfm_q.append(qg.start([2, 3, 6, 7][qtr],
                              yfm[:, 4 * qtr:4 * qtr + 4, :],
                              y_sp[4 * qtr:4 * qtr + 4]
                              .rearrange("c p t -> p c t"),
                              allow_slow_non_contiguous=True))
    m_yfm = yfm_q[3]
    wg4 = [p4.enter_context(nc.sbuf_tensor(f"wg4{i}", [P, NC, P], BF16))
           for i in range(2)]
    wu4 = [p4.enter_context(nc.sbuf_tensor(f"wu4{i}", [P, NC, P], BF16))
           for i in range(2)]
    sg4 = [p4.enter_context(nc.sbuf_tensor(f"sg4{i}", [P, 512], F32)) for i in range(2)]
    ho4 = [p4.enter_context(nc.sbuf_tensor(f"ho4{i}", [P, TOK], BF16)) for i in range(2)]
    g_ps = [p4.enter_context(nc.psum_tensor(f"g_ps{i}", [P, 512], F32))
            for i in range(2)]
    u_ps = [p4.enter_context(nc.psum_tensor(f"u_ps{i}", [P, 512], F32))
            for i in range(2)]

    wm4, hm4, sgm4, spm4 = {}, {}, {}, {}

    def load_w4(g):
        pre = [m_p3_t, m_p3_v] if g < 2 else [wm4[g - 2][2]]
        mg = qg.start(0 + 2 * (g % 2), wg4[g % 2][:].rearrange("p c j -> p (c j)"),
                      wgu_t[g].rearrange("p c j -> p (c j)"), pre=pre)
        mu = qg.start(1 + 2 * (g % 2), wu4[g % 2][:].rearrange("p c j -> p (c j)"),
                      wgu_t[g + NG].rearrange("p c j -> p (c j)"))
        wm4[g] = [mg, mu, None]

    load_w4(0)
    for g in range(NG):
        if g + 1 < NG:
            load_w4(g + 1)
        wait(T, wm4[g][0], wm4[g][1])
        if g > 0:
            wait(T, m_yfm)
        for n in range(4):
            pg, pu = g_ps[n % 2], u_ps[n % 2]
            wait(T, hm4.get((g, n - 2), hm4.get((g - 1, n + 2))))
            for c in range(NC):
                if g == 0 and c % 4 == 0:
                    wait(T, yfm_q[c // 4])
                T.matmul(pg[:], wg4[g % 2][:, c, :],
                         yfm[:, c, n * 512:(n + 1) * 512],
                         start=(c == 0), stop=(c == NC - 1))
            for c in range(NC):
                ins = T.matmul(pu[:], wu4[g % 2][:, c, :],
                               yfm[:, c, n * 512:(n + 1) * 512],
                               start=(c == 0), stop=(c == NC - 1))
            tc.inc(ins)
            m_gu = tc.mark()
            wait(Sc, m_gu, sgm4.get((g, n - 2), sgm4.get((g - 1, n + 2))))
            sc.inc(Sc.activation(sg4[n % 2][:], pg[:], AF.Silu))
            m_sg = sc.mark()
            wait(V, m_sg, spm4.get(g - 2))
            vc.inc(V.tensor_tensor(ho4[g % 2][:, n * 512:(n + 1) * 512],
                                   sg4[n % 2][:], pu[:], op=ALU.mult))
            hm4[(g, n)] = vc.mark()
            sgm4[(g, n)] = hm4[(g, n)]
        spm4[g] = qy.start(6 + g % 2, h_sp[g], ho4[g % 2][:], pre=[hm4[(g, 3)]])
        wm4[g][2] = tc.mark()
    m_p4_t, m_p4_v = tc.mark(), vc.mark()
    m_p4_sp = [spm4[NG - 2], spm4[NG - 1]]
    nc.leave_named_scope("p4", scope_p4[0], False)
    p4.close()

    if "p4" in debug:
        wait(Y, m_p4_t, m_p4_v, *m_p4_sp)
        st.close()
        return nc

    scope_p5 = nc.enter_named_scope("p5", False)
    p5 = ExitStack()
    st5 = p5.enter_context(nc.sbuf_tensor("st5", [P, 16], F32))
    p5a = ExitStack()
    hresB = p5a.enter_context(nc.sbuf_tensor("hresB", [P, NG // 2, TOK // 2], BF16))

    def hres_c(c):
        return hresA[:, c, :] if c < NG // 2 else hresB[:, c - NG // 2, :]
    ym5 = [p5a.enter_context(nc.sbuf_tensor(f"ym5{i}", [P, TOK // 2], F32))
           for i in range(2)]
    zm5 = [p5a.enter_context(nc.sbuf_tensor(f"zm5{i}", [P, TOK // 2], F32))
           for i in range(2)]
    dn_ps = [p5a.enter_context(nc.psum_tensor(f"dn_ps{i}", [P, TOK // 2], F32))
             for i in range(2)]
    tpz_ps = p5a.enter_context(nc.psum_tensor("tpz_ps", [P, TOK // 2], F32))
    ztb = [p5a.enter_context(nc.sbuf_tensor(f"ztb{i}", [P, TOK // 2], F32))
           for i in range(2)]

    wdm5, zs5, zadd5, stat5, vacc5 = {}, {}, {}, {}, {}
    tpz5, ztsp5 = {}, {}

    def load_wd(idx, m):
        pre = [] if idx < 2 else [wdm5[idx - 2][1]]
        mk = qg.start(0 + idx % 2, wd5[idx % 2][:].rearrange("p c j -> p (c j)"),
                      wd_t[m].rearrange("p c j -> p (c j)"), pre=pre)
        wdm5[idx] = [mk, None]

    sq5_m = {}

    def emit_tail5(idx, half, m):
        m_z = zadd5[idx]
        m_sq5 = sq5_m[idx]
        wait(T, m_z, tpz5.get(idx - 1))
        for tt in range(NT // 2):
            ins = T.transpose(tpz_ps[:, tt * P:(tt + 1) * P],
                              zm5[idx % 2][:, tt * P:(tt + 1) * P], ident_sb[:])
        tc.inc(ins)
        stat5[idx] = tc.mark()
        m_tpz = tc.mark()
        wait(V, m_tpz, ztsp5.get(idx - 2))
        vc.inc(V.tensor_copy(ztb[idx % 2][:], tpz_ps[:]))
        tpz5[idx] = vc.mark()
        ztsp5[idx] = qs.start(4 + idx % 2,
                              z_sp[:, m * P:(m + 1) * P]
                              .rearrange("(t p) j -> p t j", p=P)
                              [:, half * (NT // 2):(half + 1) * (NT // 2), :],
                              ztb[idx % 2][:].rearrange("p (t j) -> p t j", t=NT // 2),
                              pre=[tpz5[idx]], allow_slow_non_contiguous=True)
        zs5[idx] = ztsp5[idx]

    pend5 = None
    seqs = [(half, m) for half in range(2) for m in range(NC)]
    load_wd(0, seqs[0][1])
    m_hres, hq_marks = {}, {}
    for idx, (half, m) in enumerate(seqs):
        if m == 0 and pend5 is not None:
            emit_tail5(*pend5)
            pend5 = None
        if m == 0:
            hsl = slice(half * (TOK // 2), (half + 1) * (TOK // 2))
            hq = []
            for qtr in range(4):
                dst = (hresA[:, 16 * (qtr % 2):16 * (qtr % 2) + 16, :] if qtr < 2
                       else hresB[:, 16 * (qtr % 2):16 * (qtr % 2) + 16, :])
                if half == 0:
                    pre = [spm4[16 * qtr + 15]] if qtr < 2 else \
                        [spm4[16 * qtr + 15], m_p4_t, m_p4_v]
                else:
                    pre = [stat5[idx - 1]] if qtr == 0 else []
                hq.append(qy.start([0, 1, 4, 5][qtr], dst,
                                   h_sp[16 * qtr:16 * qtr + 16, :, hsl]
                                   .rearrange("i p t -> p i t"), pre=pre,
                                   allow_slow_non_contiguous=True))
            m_hres[half] = hq[3]
            hq_marks[half] = hq
        if idx + 1 < len(seqs):
            load_wd(idx + 1, seqs[idx + 1][1])
        m_y5 = qy.start(2 + idx % 2, ym5[idx % 2][:],
                        y_sp[m][:, half * (TOK // 2):(half + 1) * (TOK // 2)],
                        pre=([m_p4_t, m_p4_v, *m_p3_ysp] if idx < 2
                             else [zadd5[idx - 2]]))
        wait(T, wdm5[idx][0], zadd5.get(idx - 2))
        if m > 0:
            wait(T, m_hres[half])
        for n in range(2):
            for c in range(NG):
                if m == 0 and c % 16 == 0:
                    wait(T, hq_marks[half][c // 16])
                ins = T.matmul(dn_ps[idx % 2][:, n * 512:(n + 1) * 512],
                               wd5[idx % 2][:, c, :],
                               hres_c(c)[:, n * 512:(n + 1) * 512],
                               start=(c == 0), stop=(c == NG - 1))
        tc.inc(ins)
        m_dn = tc.mark()
        wdm5[idx] = [wdm5[idx][0], m_dn]
        wait(V, m_dn, m_y5, zs5.get(idx - 2), stat5.get(idx - 2))
        vc.inc(V.tensor_tensor(zm5[idx % 2][:], dn_ps[idx % 2][:],
                               ym5[idx % 2][:], op=ALU.add))
        m_z = vc.mark()
        zadd5[idx] = m_z
        sq5_m[idx] = m_z
        if pend5 is not None:
            emit_tail5(*pend5)
        pend5 = (idx, half, m)
    if pend5 is not None:
        emit_tail5(*pend5)
    m_p5a_t = tc.mark()
    m_p5a_z = [ztsp5[len(seqs) - 2], ztsp5[len(seqs) - 1]]
    m_p5a_v, m_p5a_s = vc.mark(), sc.mark()
    p5a.close()
    hstk.close()

    ot5 = [p5.enter_context(nc.sbuf_tensor(f"ot5{i}", [P, H], F32)) for i in range(2)]
    zt5 = [p5.enter_context(nc.sbuf_tensor(f"zt5{i}", [P, H], F32)) for i in range(2)]
    sqb5 = p5.enter_context(nc.sbuf_tensor("sqb5", [P, H], F32))
    store5, app5, red5, rst5 = {}, {}, {}, {}
    for t in range(NT):
        m_zc = qy.start(4 + t % 2, zt5[t % 2][:], z_sp[t * P:(t + 1) * P, :],
                        pre=(list(m_p5a_z) if t < 2 else [app5[t - 2]]))
        zz = zt5[t % 2][:]
        b = 8 * (t % 2)
        wait(V, m_zc)
        vc.inc(V.reduce_sum(st5[:, b + 0:b + 1], zz, axis=AX.X))
        m_r1 = vc.mark()
        wait(Sc, m_r1, app5.get(t - 1))
        sc.inc(Sc.activation(sqb5[:], zz, AF.Square, accum_out=st5[:, b + 1:b + 2]))
        m_r2 = sc.mark()
        wait(V, m_r2)
        V.tensor_scalar_mul(st5[:, b + 2:b + 3], st5[:, b + 0:b + 1], 1.0 / H)
        vc.inc(V.tensor_scalar_mul(st5[:, b + 3:b + 4], st5[:, b + 1:b + 2], 1.0 / H))
        wait(V, vc.mark())
        vc.inc(V.tensor_tensor(st5[:, b + 4:b + 5], st5[:, b + 2:b + 3],
                               st5[:, b + 2:b + 3], op=ALU.mult))
        wait(V, vc.mark())
        vc.inc(V.tensor_tensor(st5[:, b + 4:b + 5], st5[:, b + 3:b + 4],
                               st5[:, b + 4:b + 5], op=ALU.subtract))
        wait(V, vc.mark())
        vc.inc(V.tensor_scalar_add(st5[:, b + 4:b + 5], st5[:, b + 4:b + 5], EPS))
        wait(V, vc.mark())
        vc.inc(V.reciprocal(st5[:, b + 5:b + 6], st5[:, b + 4:b + 5]))
        m_v5 = vc.mark()
        wait(Sc, m_v5)
        sc.inc(Sc.activation(st5[:, b + 6:b + 7], st5[:, b + 5:b + 6], AF.Sqrt))
        m_rstd5 = sc.mark()
        wait(V, m_rstd5, store5.get(t - 2))
        vc.inc(V.tensor_scalar(ot5[t % 2][:], zz, st5[:, b + 2:b + 3],
                               st5[:, b + 6:b + 7],
                               op0=ALU.subtract, op1=ALU.mult))
        app5[t] = vc.mark()
        store5[t] = qy.start(6 + t % 2, out[t * P:(t + 1) * P, :], ot5[t % 2][:],
                             pre=[app5[t]])
    wait(Y, *(c.mark() for c in qy.ctrs))
    wait(Y, *(c.mark() for c in qs.ctrs))
    wait(Y, *(c.mark() for c in qg.ctrs))
    nc.leave_named_scope("p5", scope_p5[0], False)
    p5.close()
    st.close()
    return nc


def _tables(pos):
    inv = 10000.0 ** (-np.arange(0, HD, 2, dtype=np.float64) / HD)
    fr = np.asarray(pos, dtype=np.float64)[:, None] * inv[None, :]

    def arr(a):
        return np.ascontiguousarray(
            a.astype(np.float32).reshape(-1, P, 64).transpose(1, 0, 2))
    return arr(np.cos(fr)), arr(np.sin(fr))


def prep_shared(Wq, Wk, Wv, Wo, Wgu, Wd):
    def r_img(W):
        W = np.asarray(W, np.float32)
        return np.ascontiguousarray(W.T.reshape(NC, P, W.shape[0]).transpose(1, 0, 2))

    wgu_t = np.ascontiguousarray(
        np.asarray(Wgu, np.float32).reshape(NG * 2, P, NC, P).transpose(0, 3, 2, 1))
    wd_t = np.ascontiguousarray(
        np.asarray(Wd, np.float32).reshape(NC, P, NG, P).transpose(0, 3, 2, 1))
    cos_kv, sin_kv = _tables(np.arange(S))
    return dict(
        wq_r=r_img(Wq), wk_r=r_img(Wk), wv_r=r_img(Wv), wo_r=r_img(Wo),
        wgu_t=wgu_t, wd_t=wd_t, cos_kv=cos_kv, sin_kv=sin_kv,
        ident=np.eye(P, dtype=np.float32),
        ones_bf=np.ones((P, P), dtype=ml_dtypes.bfloat16),
        ones_f32=np.ones((P, 1), dtype=np.float32),
    )


def prep_core(core, x, shared):
    b = core % 4
    blocks = BLOCKS_LO if core < 4 else BLOCKS_HI
    x_b = np.asarray(x[b], np.float32)
    x_own = np.ascontiguousarray(
        np.concatenate([x_b[512 * i:512 * (i + 1)] for i in blocks], axis=0))
    pos = np.concatenate([np.arange(512 * i, 512 * (i + 1)) for i in blocks])
    cos_own, sin_own = _tables(pos)
    m = np.zeros((P, 4, WIN, 512), dtype=np.float32)
    kk = np.arange(P)[:, None]
    qq = np.arange(512)[None, :]
    for s in range(4):
        for w in range(WIN):
            kpos = P * (NKT[s] - WIN + w) + kk
            m[:, s, w, :] = (kpos <= 512 * blocks[s] + qq).astype(np.float32)
    d = dict(shared)
    d.update(x_own=x_own, x_kv=np.ascontiguousarray(x_b),
             cos_own=cos_own, sin_own=sin_own,
             maskT=m.astype(ml_dtypes.bfloat16))
    return d, b, blocks


def kernel(L0_attn_hidden_states, L0_attn_Wq, L0_attn_Wk, L0_attn_Wv,
           L0_attn_Wo, L0_ffn_w_gate_up, L0_ffn_w_down):
    x = np.asarray(L0_attn_hidden_states, np.float32)
    shared = prep_shared(L0_attn_Wq, L0_attn_Wk, L0_attn_Wv, L0_attn_Wo,
                         L0_ffn_w_gate_up, L0_ffn_w_down)
    in_maps, metas = [], []
    for c in range(8):
        d, b, blocks = prep_core(c, x, shared)
        in_maps.append(d)
        metas.append((b, blocks))
    nc = build_nc()
    res = run_bass_kernel_spmd(nc, in_maps, core_ids=list(range(8)))
    outp = np.zeros((4, S, H), dtype=np.float32)
    for c in range(8):
        b, blocks = metas[c]
        o = res.results[c]["out"]
        for k, i in enumerate(blocks):
            outp[b, 512 * i:512 * (i + 1)] = o[512 * k:512 * (k + 1)]
    return outp


# version 06 (final)
# speedup vs baseline: 1.0805x; 1.0805x over previous
import math
from contextlib import ExitStack

import numpy as np
import ml_dtypes
import concourse.bass as bass
import concourse.mybir as mybir
from concourse.bass_utils import run_bass_kernel_spmd

F32 = mybir.dt.float32
BF16 = mybir.dt.bfloat16
AF = mybir.ActivationFunctionType
ALU = mybir.AluOpType
AX = mybir.AxisListType

P = 128
S = 4096
H = 2048
NH = 16
HD = 128
NI = 8192
NC = H // P
NG = NI // P
TOK = 2048
NT = TOK // P
LKP = [4096, 3072, 2048, 1024]
NKT = [lk // P for lk in LKP]
WIN = 8
BLOCKS_LO = [7, 4, 3, 0]
BLOCKS_HI = [6, 5, 2, 1]
SCALE = 1.0 / math.sqrt(HD)
SHIFT = -30.0
EPS = 1e-5


class Ctr:
    def __init__(self, stack, nc, name):
        self.sem = stack.enter_context(nc.semaphore(name))
        self.n = 0

    def inc(self, instr, k=1):
        instr.then_inc(self.sem, k)
        self.n += k
        return instr

    def mark(self):
        return (self.sem, self.n)


def wait(eng, *marks):
    for m in marks:
        if m is None:
            continue
        sem, n = m
        if n > 0:
            eng.wait_ge(sem, n)


class DmaQ:

    def __init__(self, stack, nc, name, eng, nslots):
        self.eng = eng
        self.ctrs = [Ctr(stack, nc, f"{name}{i}") for i in range(nslots)]

    def start(self, slot, out, in_, pre=(), transpose=False, **kw):
        c = self.ctrs[slot]
        wait(self.eng, *pre)
        if transpose:
            c.inc(self.eng.dma_start_transpose(out, in_), 16)
        else:
            c.inc(self.eng.dma_start(out, in_, **kw), 16)
        return c.mark()


def build_nc(debug=()):
    nc = bass.Bass("TRN2", target_bir_lowering=False, detect_race_conditions=False)
    st = ExitStack()

    def dram(name, shape, dtype, kind="ExternalInput"):
        return nc.dram_tensor(name, shape, dtype, kind=kind).ap()

    x_own = dram("x_own", [TOK, H], F32)
    x_kv = dram("x_kv", [S, H], F32)
    wq_r = dram("wq_r", [P, NC, H], F32)
    wk_r = dram("wk_r", [P, NC, H], F32)
    wv_r = dram("wv_r", [P, NC, H], F32)
    wo_r = dram("wo_r", [P, NH, H], F32)
    wgu_t = dram("wgu_t", [NG * 2, P, NC, P], F32)
    wd_t = dram("wd_t", [NC, P, NG, P], F32)
    cos_own = dram("cos_own", [P, NT, 64], F32)
    sin_own = dram("sin_own", [P, NT, 64], F32)
    cos_kv = dram("cos_kv", [P, S // P, 64], F32)
    sin_kv = dram("sin_kv", [P, S // P, 64], F32)
    maskT = dram("maskT", [P, 4, WIN, 512], BF16)
    ident = dram("ident", [P, P], F32)
    ones_bf = dram("ones_bf", [P, P], BF16)
    ones_f32 = dram("ones_f32", [P, 1], F32)

    out = dram("out", [TOK, H], F32, kind="ExternalOutput")

    def scr(name, shape, dtype):
        kind = "ExternalOutput" if name in debug else "Internal"
        return dram(name, shape, dtype, kind=kind)

    xbf_kv = scr("xbf_kv", [S, H], BF16)
    xbf_own = scr("xbf_own", [TOK, H], BF16)
    k_sp = scr("k_sp", [S, H], BF16)
    v_sp = scr("v_sp", [S, H], BF16)
    q_sp = scr("q_sp", [TOK, H], BF16)
    y_sp = scr("y_sp", [NC, P, TOK], F32)
    h_sp = scr("h_sp", [NG, P, TOK], BF16)
    z_sp = scr("z_sp", [TOK, H], F32)
    ctx_dump = scr("ctx_dump", [P, NH * TOK], BF16) if "ctx_dump" in debug else None
    st1_dump = scr("st1_dump", [NT, P, 8], F32) if "p3z" in debug else None

    T, V, Sc, G, Y = nc.tensor, nc.vector, nc.scalar, nc.gpsimd, nc.sync
    tc = Ctr(st, nc, "tC")
    vc = Ctr(st, nc, "vC")
    sc = Ctr(st, nc, "sC")
    setup = Ctr(st, nc, "setup")
    setup_g = Ctr(st, nc, "setupG")
    qy = DmaQ(st, nc, "qy", Y, 8)
    qg = DmaQ(st, nc, "qg", G, 8)
    qs = DmaQ(st, nc, "qs", Sc, 8)

    def sb(name, shape, dtype):
        return st.enter_context(nc.sbuf_tensor(name, shape, dtype))

    ident_sb = sb("ident_sb", [P, P], F32)
    ones_sb = sb("ones_sb", [P, P], BF16)
    ones1_sb = sb("ones1_sb", [P, 1], F32)
    consts = sb("consts", [P, 2], F32)
    setup.inc(Y.dma_start(ident_sb[:], ident[:]), 16)
    setup.inc(Y.dma_start(ones_sb[:], ones_bf[:]), 16)
    setup.inc(Y.dma_start(ones1_sb[:], ones_f32[:]), 16)
    V.memset(consts[:, 0:1], SCALE)
    vc.inc(V.memset(consts[:, 1:2], SHIFT))
    m_consts = vc.mark()
    c_scale, c_shift = consts[:, 0:1], consts[:, 1:2]

    p1 = ExitStack()
    w_a = p1.enter_context(nc.sbuf_tensor("w_a", [P, NC, H], BF16))
    wvstk = ExitStack()
    w_b = wvstk.enter_context(nc.sbuf_tensor("w_b", [P, NC, H], BF16, side="right"))
    tab = p1.enter_context(nc.sbuf_tensor("tab", [P, 2, 48, 64], F32))
    xfm = [p1.enter_context(nc.sbuf_tensor(f"xfm{i}", [P, NC, 512], BF16))
           for i in range(2)]
    rsc = p1.enter_context(nc.sbuf_tensor("rsc", [P, 4, NH // 2, 64], F32))
    cast_ctrs = {}
    for nm, nst in (("own", TOK // 512), ("kv", S // 512)):
        for j in range(nst):
            cast_ctrs[(nm, j)] = Ctr(st, nc, f"xc_{nm}{j}")

    def issue_cast(nm, j):
        srcap = x_own if nm == "own" else x_kv
        dstap = xbf_own if nm == "own" else xbf_kv
        cast_ctrs[(nm, j)].inc(
            G.dma_start(dstap[j * 512:(j + 1) * 512, :],
                        srcap[j * 512:(j + 1) * 512, :]), 16)

    po = [p1.enter_context(nc.sbuf_tensor(f"po{i}", [P, H], BF16)) for i in range(2)]
    mm_h = [[p1.enter_context(nc.psum_tensor(f"mm_h{i}_{pp}", [P, H // 2], F32))
             for pp in range(2)] for i in range(2)]

    setup.inc(Y.dma_start(tab[:, 0, 0:16, :], cos_own[:]), 16)
    setup.inc(Y.dma_start(tab[:, 1, 0:16, :], sin_own[:]), 16)
    setup.inc(Y.dma_start(tab[:, 0, 16:48, :], cos_kv[:]), 16)
    setup.inc(Y.dma_start(tab[:, 1, 16:48, :], sin_kv[:]), 16)
    m_setup = setup.mark()

    scope_p1 = nc.enter_named_scope("p1", False)
    state = dict(prev_evac=None, prev_rope=None, obuf=0, spill={},
                 hfree={}, tload={}, mmdone={})

    issue_cast("kv", 0)
    wk_q = []
    for qtr in range(4):
        wk_q.append(qg.start([2, 3, 6, 7][qtr],
                             w_a[:, 4 * qtr:4 * qtr + 4, :]
                             .rearrange("p c o -> p (c o)"),
                             wk_r[:, 4 * qtr:4 * qtr + 4, :]
                             .rearrange("p c o -> p (c o)")))
    m_wk = wk_q[3]
    m_wv = qg.start(1, w_b[:].rearrange("p c o -> p (c o)"),
                    wv_r[:].rearrange("p c o -> p (c o)"))
    for j in range(1, S // 512):
        issue_cast("kv", j)
    for j in range(TOK // 512):
        issue_cast("own", j)

    def p1_stripe(si, nm, stripe, tabbase, wlist):
        b = si % 2
        xbf = xbf_own if nm == "own" else xbf_kv
        pre = [cast_ctrs[(nm, stripe)].mark(), state["mmdone"].get(si - 2)]
        for c in range(NC):
            mT = qy.start(b, xfm[b][:, c, :],
                          xbf[stripe * 512:(stripe + 1) * 512,
                              c * P:(c + 1) * P],
                          pre=(pre if c == 0 else ()), transpose=True)
        state["tload"][si] = mT
        for j in range(4):
            tile = stripe * 4 + j
            for (w_sb, m_w, do_rope, sp, sp_row) in wlist:
                qmarks = m_w if isinstance(m_w, list) else None
                ob = state["obuf"] % 2
                state["obuf"] += 1
                pp = (state["obuf"] - 1) % 2
                for half in range(2):
                    hm = half * (NH // 2)
                    wait(T, mT, state["hfree"].get((half, pp)))
                    if qmarks is None:
                        wait(T, m_w)
                    mh = mm_h[half][pp]
                    for n in (0, 1):
                        nn = 2 * half + n
                        for c in range(NC):
                            if qmarks is not None and c % 4 == 0:
                                wait(T, qmarks[c // 4])
                            ins = T.matmul(mh[:, n * 512:(n + 1) * 512],
                                           xfm[b][:, c, j * P:(j + 1) * P],
                                           w_sb[:, c, nn * 512:(nn + 1) * 512],
                                           start=(c == 0), stop=(c == NC - 1))
                    tc.inc(ins)
                    m_mm = tc.mark()
                    wait(V, m_mm, m_consts)
                    if half == 0:
                        wait(V, state["spill"].get(ob))
                    pv = mh[:].rearrange("p (h j) -> p h j", h=NH // 2)
                    ov = po[ob][:].rearrange("p (h j) -> p h j", h=NH)[:,
                                                                      hm:hm + NH // 2]
                    if do_rope:
                        cosb = tab[:, 0, tabbase + tile, :].unsqueeze(1) \
                            .broadcast_to([P, NH // 2, 64])
                        sinb = tab[:, 1, tabbase + tile, :].unsqueeze(1) \
                            .broadcast_to([P, NH // 2, 64])
                        rv = rsc[:]
                        V.tensor_tensor(rv[:, 0], pv[:, :, 0:64], cosb, op=ALU.mult)
                        V.tensor_tensor(rv[:, 1], pv[:, :, 64:128], sinb, op=ALU.mult)
                        V.tensor_tensor(rv[:, 2], pv[:, :, 64:128], cosb, op=ALU.mult)
                        V.tensor_tensor(rv[:, 3], pv[:, :, 0:64], sinb, op=ALU.mult)
                        V.tensor_tensor(ov[:, :, 0:64], rv[:, 0], rv[:, 1],
                                        op=ALU.subtract)
                        vc.inc(V.tensor_tensor(ov[:, :, 64:128], rv[:, 2], rv[:, 3],
                                               op=ALU.add))
                    else:
                        wait(Sc, m_mm)
                        if half == 0:
                            wait(Sc, state["spill"].get(ob))
                        sc.inc(Sc.copy(
                            po[ob][:, half * (H // 2):(half + 1) * (H // 2)],
                            mh[:]))
                    state["hfree"][(half, pp)] = (sc.mark() if not do_rope
                                                  else vc.mark())
                state["prev_rope"] = vc.mark() if do_rope else sc.mark()
                state["spill"][ob] = qs.start(state["obuf"] % 4,
                                              sp[tile * P:(tile + 1) * P, :],
                                              po[ob][:], pre=[state["prev_rope"]])
        state["mmdone"][si] = tc.mark()

    for sj in range(S // 512):
        mk_w = wk_q if sj == 0 else m_wk
        p1_stripe(sj, "kv", sj, 16,
                  [(w_a, mk_w, True, k_sp, None), (w_b, m_wv, False, v_sp, None)])
    m_kv_done = tc.mark()
    m_kv_sp = [c.mark() for c in qs.ctrs]
    wvstk.close()
    pfstk = ExitStack()
    kfm = [pfstk.enter_context(nc.sbuf_tensor(f"kfm{i}", [P, S], BF16,
                                              side="right")) for i in range(2)]
    qfm = [pfstk.enter_context(nc.sbuf_tensor(f"qfm{i}", [P, TOK], BF16,
                                              side="right")) for i in range(2)]
    vtm = [pfstk.enter_context(nc.sbuf_tensor(f"vtm{i}", [P, S // P, HD], BF16,
                                              side="right")) for i in range(2)]
    probs = [pfstk.enter_context(nc.sbuf_tensor(f"probs{i}", [P, 512], BF16,
                                                side="right")) for i in range(4)]
    recip = [pfstk.enter_context(nc.sbuf_tensor(f"recip{i}", [P, 512], F32,
                                                side="right")) for i in range(2)]
    v_view = v_sp[:].rearrange("(t p) c -> p t c", p=P)
    head_kv_marks = {}

    def load_head_kv(h, pre):
        mk = qy.start(2 + 2 * (h % 2), kfm[h % 2][:],
                      k_sp[:, h * HD:(h + 1) * HD], pre=pre, transpose=True)
        mv = qg.start(4 + (h % 2), vtm[h % 2][:],
                      v_view[:, :, h * HD:(h + 1) * HD],
                      pre=pre, allow_slow_non_contiguous=True)
        head_kv_marks[h] = (mk, mv)

    wq_q = []
    for qtr in range(4):
        wq_q.append(qg.start([2, 3, 6, 7][qtr],
                             w_a[:, 4 * qtr:4 * qtr + 4, :]
                             .rearrange("p c o -> p (c o)"),
                             wq_r[:, 4 * qtr:4 * qtr + 4, :]
                             .rearrange("p c o -> p (c o)"),
                             pre=([m_kv_done] if qtr == 0 else [])))
    m_wq = wq_q[3]
    for si in range(TOK // 512):
        mq_w = wq_q if si == 0 else m_wq
        p1_stripe(S // 512 + si, "own", si, 0, [(w_a, mq_w, True, q_sp, None)])
        if si == 0:
            load_head_kv(0, list(m_kv_sp))
            load_head_kv(1, list(m_kv_sp))

    m_p1_t, m_p1_v = tc.mark(), vc.mark()
    m_p1_sp = [c.mark() for c in qs.ctrs]
    nc.leave_named_scope("p1", scope_p1[0], False)
    p1.close()

    if "p1" in debug:
        wait(Y, m_p1_t, m_p1_v, *m_p1_sp)
        st.close()
        return nc

    scope_p2 = nc.enter_named_scope("p2", False)
    ctxstk = ExitStack()
    ctx_res = ctxstk.enter_context(nc.sbuf_tensor("ctx_res", [P, NH, TOK], BF16))
    p2 = ExitStack()
    mask_sb = p2.enter_context(nc.sbuf_tensor("mask_sb", [P, 4 * WIN * 512], BF16))
    sc_ps = [p2.enter_context(nc.psum_tensor(f"sc_ps{i}", [P, 512], F32))
             for i in range(3)]
    ctx_ps = [p2.enter_context(nc.psum_tensor(f"ctx_ps{i}", [P, 512], F32))
              for i in range(2)]
    sum_ps = [p2.enter_context(nc.psum_tensor(f"sum_ps{i}", [P, 512], F32))
              for i in range(2)]

    mask_ctr = Ctr(st, nc, "maskL")
    wait(G, m_p1_t, m_p1_v, *m_p1_sp)
    mask_ctr.inc(G.dma_start(mask_sb[:],
                             maskT[:].rearrange("p a b c -> p (a b c)")), 16)
    m_mask = mask_ctr.mark()

    head_marks = {}
    head_done = {}

    def load_head(h):
        if h < 2:
            mk, mv = head_kv_marks[h]
            mq = qy.start(3 + 2 * (h % 2), qfm[h % 2][:],
                          q_sp[:, h * HD:(h + 1) * HD],
                          pre=list(m_p1_sp), transpose=True)
        else:
            pre = [head_done[h - 2]]
            mk = qy.start(2 + 2 * (h % 2), kfm[h % 2][:],
                          k_sp[:, h * HD:(h + 1) * HD], pre=pre, transpose=True)
            mq = qy.start(3 + 2 * (h % 2), qfm[h % 2][:],
                          q_sp[:, h * HD:(h + 1) * HD], transpose=True)
            mv = qg.start(4 + (h % 2), vtm[h % 2][:],
                          v_view[:, :, h * HD:(h + 1) * HD],
                          pre=pre, allow_slow_non_contiguous=True)
        head_marks[h] = (mk, mq, mv)

    load_head(0)
    load_head(1)
    exp_m, pv_m, evac_m = {}, {}, {}
    items = []
    for h in range(NH):
        for s in range(4):
            for kt in range(NKT[s]):
                items.append((len(items), h, s, kt))

    def emit_qk(k, h, s, kt):
        wait(T, exp_m.get(k - 3))
        tc.inc(T.matmul(sc_ps[k % 3][:],
                        kfm[h % 2][:, kt * P:(kt + 1) * P],
                        qfm[h % 2][:, s * 512:(s + 1) * 512],
                        start=True, stop=True))
        m_qk = tc.mark()
        wait(Sc, pv_m.get(k - 4), m_qk)
        sc.inc(Sc.activation(probs[k % 4][:], sc_ps[k % 3][:], AF.Exp,
                             bias=c_shift, scale=c_scale))
        exp_m[k] = sc.mark()
        dep = exp_m[k]
        if kt >= NKT[s] - WIN:
            w = kt - (NKT[s] - WIN)
            wait(V, exp_m[k], m_mask)
            msl = mask_sb[:, (s * WIN + w) * 512:(s * WIN + w + 1) * 512]
            vc.inc(V.tensor_tensor(probs[k % 4][:], probs[k % 4][:], msl,
                                   op=ALU.mult))
            dep = vc.mark()
        return dep

    def emit_pv(k, h, s, kt, dep):
        sl = 4 * h + s
        first, last = kt == 0, kt == NKT[s] - 1
        wait(T, dep)
        if first:
            wait(T, evac_m.get(sl - 2))
        T.matmul(ctx_ps[sl % 2][:], vtm[h % 2][:, kt, :], probs[k % 4][:],
                 start=first, stop=last)
        tc.inc(T.matmul(sum_ps[sl % 2][:], ones_sb[:], probs[k % 4][:],
                        start=first, stop=last))
        pv_m[k] = tc.mark()
        if last:
            wait(V, pv_m[k])
            V.reciprocal(recip[sl % 2][:], sum_ps[sl % 2][:])
            vc.inc(V.tensor_tensor(ctx_res[:, h, s * 512:(s + 1) * 512],
                                   ctx_ps[sl % 2][:], recip[sl % 2][:],
                                   op=ALU.mult))
            evac_m[sl] = vc.mark()
            if s == 3:
                head_done[h] = tc.mark()

    from collections import deque
    pend = deque()
    for (k, h, s, kt) in items:
        if s == 0 and kt == 0:
            while pend:
                emit_pv(*pend.popleft())
            if h + 1 < NH:
                load_head(h + 1)
            wait(T, *head_marks[h], m_setup)
        dep = emit_qk(k, h, s, kt)
        pend.append((k, h, s, kt, dep))
        if len(pend) > 2:
            emit_pv(*pend.popleft())
    while pend:
        emit_pv(*pend.popleft())
    m_p2_t, m_p2_v = tc.mark(), vc.mark()
    nc.leave_named_scope("p2", scope_p2[0], False)
    p2.close()
    pfstk.close()

    if "p2" in debug:
        wait(Y, m_p2_v)
        fin = qy.start(0, ctx_dump[:],
                       ctx_res[:].rearrange("p h t -> p (h t)"), pre=[m_p2_v])
        wait(Y, fin)
        ctxstk.close()
        st.close()
        return nc

    p3 = ExitStack()
    wo_sb = p3.enter_context(nc.sbuf_tensor("wo_sb", [P, NH, H], BF16))
    x3 = [p3.enter_context(nc.sbuf_tensor(f"x3{i}", [P, H], F32)) for i in range(2)]
    z1 = p3.enter_context(nc.sbuf_tensor("z1", [P, H], F32))
    yt = [p3.enter_context(nc.sbuf_tensor(f"yt{i}", [P, H], F32))
          for i in range(2)]
    yspb = [p3.enter_context(nc.sbuf_tensor("yspb0", [P, H], F32))]
    st1 = p3.enter_context(nc.sbuf_tensor("st1", [P, 8], F32))
    o_h = [p3.enter_context(nc.psum_tensor(f"o_h{i}", [P, H // 2], F32))
           for i in range(2)]
    tp3_ps = p3.enter_context(nc.psum_tensor("tp3_ps", [P, H], F32))

    scope_p3 = nc.enter_named_scope("p3", False)
    wo_ctr = Ctr(st, nc, "woL")
    wait(G, m_p2_t, m_p2_v)
    wo_q = []
    for qtr in range(4):
        wo_q.append(qg.start([2, 3, 6, 7][qtr],
                             wo_sb[:, 4 * qtr:4 * qtr + 4, :]
                             .rearrange("p h o -> p (h o)"),
                             wo_r[:, 4 * qtr:4 * qtr + 4, :]
                             .rearrange("p h o -> p (h o)")))
    m_wo = wo_q[3]

    zadd3, ysp3, tpev3 = {}, {}, None
    addh3, app3 = {}, {}
    pend3 = None

    def emit_o(t):
        m_x = qy.start(t % 2, x3[t % 2][:], x_own[t * P:(t + 1) * P, :],
                       pre=([m_p2_t, m_p2_v] if t < 2 else [app3[t - 2]]))
        for half in range(2):
            wait(T, m_p2_v if t == 0 else None, addh3.get((t - 1, half)))
            for n in (2 * half, 2 * half + 1):
                for h in range(NH):
                    if t < 2 and h % 4 == 0:
                        wait(T, wo_q[h // 4])
                    ins = T.matmul(o_h[half][:, (n - 2 * half) * 512:
                                              (n - 2 * half + 1) * 512],
                                   ctx_res[:, h, t * P:(t + 1) * P],
                                   wo_sb[:, h, n * 512:(n + 1) * 512],
                                   start=(h == 0), stop=(h == NH - 1))
            tc.inc(ins)
            m_omm = tc.mark()
            wait(V, m_omm, m_x, app3.get(t - 1) if half == 0 else None)
            vc.inc(V.tensor_tensor(z1[:, half * 1024:(half + 1) * 1024],
                                   o_h[half][:],
                                   x3[t % 2][:, half * 1024:(half + 1) * 1024],
                                   op=ALU.add))
            addh3[(t, half)] = vc.mark()
        zadd3[t] = addh3[(t, 1)]
        vc.inc(V.reduce_sum(st1[:, 0:1], z1[:], axis=AX.X))
        m_z1 = vc.mark()
        wait(Sc, m_z1, emit_o.tpmark.get(t - 2))
        sc.inc(Sc.activation(yt[t % 2][:], z1[:], AF.Square, accum_out=st1[:, 1:2]))
        m_sq = sc.mark()
        wait(V, m_sq)
        V.tensor_scalar_mul(st1[:, 2:3], st1[:, 0:1], 1.0 / H)
        vc.inc(V.tensor_scalar_mul(st1[:, 3:4], st1[:, 1:2], 1.0 / H))
        wait(V, vc.mark())
        vc.inc(V.tensor_tensor(st1[:, 4:5], st1[:, 2:3], st1[:, 2:3], op=ALU.mult))
        wait(V, vc.mark())
        vc.inc(V.tensor_tensor(st1[:, 4:5], st1[:, 3:4], st1[:, 4:5],
                               op=ALU.subtract))
        wait(V, vc.mark())
        vc.inc(V.tensor_scalar_add(st1[:, 4:5], st1[:, 4:5], EPS))
        wait(V, vc.mark())
        vc.inc(V.reciprocal(st1[:, 5:6], st1[:, 4:5]))
        m_v1 = vc.mark()
        wait(Sc, m_v1)
        sc.inc(Sc.activation(st1[:, 6:7], st1[:, 5:6], AF.Sqrt))
        m_rstd = sc.mark()
        wait(V, m_rstd)
        vc.inc(V.tensor_scalar(yt[t % 2][:], z1[:], st1[:, 2:3], st1[:, 6:7],
                               op0=ALU.subtract, op1=ALU.mult))
        app3[t] = vc.mark()
        return app3[t]

    def emit_ytp(t, m_y):
        wait(T, m_y, emit_ytp.tpev)
        for c in range(NC):
            ins = T.transpose(tp3_ps[:, c * P:(c + 1) * P],
                              yt[t % 2][:, c * P:(c + 1) * P], ident_sb[:])
        tc.inc(ins)
        m_tp = tc.mark()
        emit_o.tpmark[t] = m_tp
        wait(V, m_tp, ysp3.get(t - 1))
        vc.inc(V.tensor_copy(yspb[0][:], tp3_ps[:]))
        m_ye = vc.mark()
        emit_ytp.tpev = m_ye
        ysp3[t] = qy.start(4 + t % 2,
                           y_sp[:, :, t * P:(t + 1) * P].rearrange("c p j -> p c j"),
                           yspb[0][:].rearrange("p (c j) -> p c j", c=NC),
                           pre=[m_ye], allow_slow_non_contiguous=True)

    emit_ytp.tpev = None
    emit_o.tpmark = {}
    for t in range(NT):
        m_y = emit_o(t)
        if pend3 is not None:
            emit_ytp(*pend3)
        pend3 = (t, m_y)
    emit_ytp(*pend3)
    m_p3_t, m_p3_v, m_p3_s = tc.mark(), vc.mark(), sc.mark()
    m_p3_ysp = [ysp3[t] for t in (NT - 2, NT - 1)]
    nc.leave_named_scope("p3", scope_p3[0], False)
    p3.close()
    ctxstk.close()

    if "p3" in debug:
        wait(Y, m_p3_t, m_p3_v, m_p3_s, *m_p3_ysp)
        st.close()
        return nc

    hstk = ExitStack()
    hresA = hstk.enter_context(nc.sbuf_tensor("hresA", [P, NG // 2, TOK // 2], BF16,
                                              side="right"))
    wd5 = [hstk.enter_context(nc.sbuf_tensor(f"wd5{i}", [P, NG, P], BF16,
                                             side="right")) for i in range(2)]
    scope_p4 = nc.enter_named_scope("p4", False)
    p4 = ExitStack()
    yfm = p4.enter_context(nc.sbuf_tensor("yfm", [P, NC, TOK], BF16))
    wait(G, m_p3_t, m_p3_v, m_p3_s, *m_p3_ysp)
    _yfm_pending = [True]
    wg4 = [p4.enter_context(nc.sbuf_tensor(f"wg4{i}", [P, NC, P], BF16))
           for i in range(2)]
    wu4 = [p4.enter_context(nc.sbuf_tensor(f"wu4{i}", [P, NC, P], BF16))
           for i in range(2)]
    sg4 = [p4.enter_context(nc.sbuf_tensor(f"sg4{i}", [P, 512], F32)) for i in range(2)]
    ho4 = [p4.enter_context(nc.sbuf_tensor(f"ho4{i}", [P, TOK], BF16)) for i in range(2)]
    g_ps = [p4.enter_context(nc.psum_tensor(f"g_ps{i}", [P, 512], F32))
            for i in range(2)]
    u_ps = [p4.enter_context(nc.psum_tensor(f"u_ps{i}", [P, 512], F32))
            for i in range(2)]

    wm4, hm4, sgm4, spm4 = {}, {}, {}, {}

    def load_w4(g):
        pre = [m_p3_t, m_p3_v] if g < 2 else [wm4[g - 2][2]]
        mg = qg.start(0 + 2 * (g % 2), wg4[g % 2][:].rearrange("p c j -> p (c j)"),
                      wgu_t[g].rearrange("p c j -> p (c j)"), pre=pre)
        mu = qg.start(1 + 2 * (g % 2), wu4[g % 2][:].rearrange("p c j -> p (c j)"),
                      wgu_t[g + NG].rearrange("p c j -> p (c j)"))
        wm4[g] = [mg, mu, None]

    load_w4(0)
    load_w4(1)
    yfm_q = []
    for half_ in range(2):
        yfm_q.append(qg.start(6 + half_,
                              yfm[:, 8 * half_:8 * half_ + 8, :],
                              y_sp[8 * half_:8 * half_ + 8]
                              .rearrange("c p t -> p c t"),
                              allow_slow_non_contiguous=True))
    m_yfm = yfm_q[1]
    for g in range(NG):
        if g + 1 < NG and g >= 1:
            load_w4(g + 1)
        wait(T, wm4[g][0], wm4[g][1])
        if g > 0:
            wait(T, m_yfm)
        for n in range(4):
            pg, pu = g_ps[n % 2], u_ps[n % 2]
            wait(T, hm4.get((g, n - 2), hm4.get((g - 1, n + 2))))
            for c in range(NC):
                if g == 0 and c % 8 == 0:
                    wait(T, yfm_q[c // 8])
                T.matmul(pg[:], wg4[g % 2][:, c, :],
                         yfm[:, c, n * 512:(n + 1) * 512],
                         start=(c == 0), stop=(c == NC - 1))
            for c in range(NC):
                ins = T.matmul(pu[:], wu4[g % 2][:, c, :],
                               yfm[:, c, n * 512:(n + 1) * 512],
                               start=(c == 0), stop=(c == NC - 1))
            tc.inc(ins)
            m_gu = tc.mark()
            wait(Sc, m_gu, sgm4.get((g, n - 2), sgm4.get((g - 1, n + 2))))
            sc.inc(Sc.activation(sg4[n % 2][:], pg[:], AF.Silu))
            m_sg = sc.mark()
            wait(V, m_sg, spm4.get(g - 2))
            vc.inc(V.tensor_tensor(ho4[g % 2][:, n * 512:(n + 1) * 512],
                                   sg4[n % 2][:], pu[:], op=ALU.mult))
            hm4[(g, n)] = vc.mark()
            sgm4[(g, n)] = hm4[(g, n)]
        spm4[g] = qy.start(6 + g % 2, h_sp[g], ho4[g % 2][:], pre=[hm4[(g, 3)]])
        wm4[g][2] = tc.mark()
    m_p4_t, m_p4_v = tc.mark(), vc.mark()
    m_p4_sp = [spm4[NG - 2], spm4[NG - 1]]
    nc.leave_named_scope("p4", scope_p4[0], False)
    p4.close()

    if "p4" in debug:
        wait(Y, m_p4_t, m_p4_v, *m_p4_sp)
        st.close()
        return nc

    scope_p5 = nc.enter_named_scope("p5", False)
    p5 = ExitStack()
    st5 = p5.enter_context(nc.sbuf_tensor("st5", [P, 16], F32))
    p5a = ExitStack()
    hresB = p5a.enter_context(nc.sbuf_tensor("hresB", [P, NG // 2, TOK // 2], BF16))

    def hres_c(c):
        return hresA[:, c, :] if c < NG // 2 else hresB[:, c - NG // 2, :]
    ym5 = [p5a.enter_context(nc.sbuf_tensor(f"ym5{i}", [P, TOK // 2], F32))
           for i in range(2)]
    zm5 = [p5a.enter_context(nc.sbuf_tensor(f"zm5{i}", [P, TOK // 2], F32))
           for i in range(2)]
    dn_ps = [p5a.enter_context(nc.psum_tensor(f"dn_ps{i}", [P, TOK // 2], F32))
             for i in range(2)]
    tpz_ps = p5a.enter_context(nc.psum_tensor("tpz_ps", [P, TOK // 2], F32))
    ztb = [p5a.enter_context(nc.sbuf_tensor(f"ztb{i}", [P, TOK // 2], F32))
           for i in range(2)]

    wdm5, zs5, zadd5, stat5, vacc5 = {}, {}, {}, {}, {}
    tpz5, ztsp5 = {}, {}

    def load_wd(idx, m):
        pre = [] if idx < 2 else [wdm5[idx - 2][1]]
        mk = qg.start(0 + idx % 2, wd5[idx % 2][:].rearrange("p c j -> p (c j)"),
                      wd_t[m].rearrange("p c j -> p (c j)"), pre=pre)
        wdm5[idx] = [mk, None]

    sq5_m = {}

    def emit_tail5(idx, half, m):
        m_z = zadd5[idx]
        m_sq5 = sq5_m[idx]
        wait(T, m_z, tpz5.get(idx - 1))
        for tt in range(NT // 2):
            ins = T.transpose(tpz_ps[:, tt * P:(tt + 1) * P],
                              zm5[idx % 2][:, tt * P:(tt + 1) * P], ident_sb[:])
        tc.inc(ins)
        stat5[idx] = tc.mark()
        m_tpz = tc.mark()
        wait(V, m_tpz, ztsp5.get(idx - 2))
        vc.inc(V.tensor_copy(ztb[idx % 2][:], tpz_ps[:]))
        tpz5[idx] = vc.mark()
        ztsp5[idx] = qs.start(4 + idx % 2,
                              z_sp[:, m * P:(m + 1) * P]
                              .rearrange("(t p) j -> p t j", p=P)
                              [:, half * (NT // 2):(half + 1) * (NT // 2), :],
                              ztb[idx % 2][:].rearrange("p (t j) -> p t j", t=NT // 2),
                              pre=[tpz5[idx]], allow_slow_non_contiguous=True)
        zs5[idx] = ztsp5[idx]

    pend5 = None
    seqs = [(half, m) for half in range(2) for m in range(NC)]
    load_wd(0, seqs[0][1])
    m_hres, hq_marks = {}, {}
    for idx, (half, m) in enumerate(seqs):
        if m == 0 and pend5 is not None:
            emit_tail5(*pend5)
            pend5 = None
        if m == 0:
            hsl = slice(half * (TOK // 2), (half + 1) * (TOK // 2))
            hq = []
            for qtr in range(4):
                dst = (hresA[:, 16 * (qtr % 2):16 * (qtr % 2) + 16, :] if qtr < 2
                       else hresB[:, 16 * (qtr % 2):16 * (qtr % 2) + 16, :])
                if half == 0:
                    pre = [spm4[16 * qtr + 15]] if qtr < 2 else \
                        [spm4[16 * qtr + 15], m_p4_t, m_p4_v]
                else:
                    pre = [stat5[idx - 1]] if qtr == 0 else []
                hq.append(qy.start([0, 1, 4, 5][qtr], dst,
                                   h_sp[16 * qtr:16 * qtr + 16, :, hsl]
                                   .rearrange("i p t -> p i t"), pre=pre,
                                   allow_slow_non_contiguous=True))
            m_hres[half] = hq[3]
            hq_marks[half] = hq
        if idx + 1 < len(seqs):
            load_wd(idx + 1, seqs[idx + 1][1])
        m_y5 = qy.start(2 + idx % 2, ym5[idx % 2][:],
                        y_sp[m][:, half * (TOK // 2):(half + 1) * (TOK // 2)],
                        pre=([m_p4_t, m_p4_v, *m_p3_ysp] if idx < 2
                             else [zadd5[idx - 2]]))
        wait(T, wdm5[idx][0], zadd5.get(idx - 2))
        if m > 0:
            wait(T, m_hres[half])
        for n in range(2):
            for c in range(NG):
                if m == 0 and c % 16 == 0:
                    wait(T, hq_marks[half][c // 16])
                ins = T.matmul(dn_ps[idx % 2][:, n * 512:(n + 1) * 512],
                               wd5[idx % 2][:, c, :],
                               hres_c(c)[:, n * 512:(n + 1) * 512],
                               start=(c == 0), stop=(c == NG - 1))
        tc.inc(ins)
        m_dn = tc.mark()
        wdm5[idx] = [wdm5[idx][0], m_dn]
        wait(V, m_dn, m_y5, zs5.get(idx - 2), stat5.get(idx - 2))
        vc.inc(V.tensor_tensor(zm5[idx % 2][:], dn_ps[idx % 2][:],
                               ym5[idx % 2][:], op=ALU.add))
        m_z = vc.mark()
        zadd5[idx] = m_z
        sq5_m[idx] = m_z
        if pend5 is not None:
            emit_tail5(*pend5)
        pend5 = (idx, half, m)
    if pend5 is not None:
        emit_tail5(*pend5)
    m_p5a_t = tc.mark()
    m_p5a_z = [ztsp5[len(seqs) - 2], ztsp5[len(seqs) - 1]]
    m_p5a_v, m_p5a_s = vc.mark(), sc.mark()
    p5a.close()
    hstk.close()

    ot5 = [p5.enter_context(nc.sbuf_tensor(f"ot5{i}", [P, H], F32)) for i in range(2)]
    zt5 = [p5.enter_context(nc.sbuf_tensor(f"zt5{i}", [P, H], F32)) for i in range(2)]
    sqb5 = p5.enter_context(nc.sbuf_tensor("sqb5", [P, H], F32))
    store5, app5, red5, rst5 = {}, {}, {}, {}
    for t in range(NT):
        m_zc = qy.start(4 + t % 2, zt5[t % 2][:], z_sp[t * P:(t + 1) * P, :],
                        pre=(list(m_p5a_z) if t < 2 else [app5[t - 2]]))
        zz = zt5[t % 2][:]
        b = 8 * (t % 2)
        wait(V, m_zc)
        vc.inc(V.reduce_sum(st5[:, b + 0:b + 1], zz, axis=AX.X))
        m_r1 = vc.mark()
        wait(Sc, m_r1, app5.get(t - 1))
        sc.inc(Sc.activation(sqb5[:], zz, AF.Square, accum_out=st5[:, b + 1:b + 2]))
        m_r2 = sc.mark()
        wait(V, m_r2)
        V.tensor_scalar_mul(st5[:, b + 2:b + 3], st5[:, b + 0:b + 1], 1.0 / H)
        vc.inc(V.tensor_scalar_mul(st5[:, b + 3:b + 4], st5[:, b + 1:b + 2], 1.0 / H))
        wait(V, vc.mark())
        vc.inc(V.tensor_tensor(st5[:, b + 4:b + 5], st5[:, b + 2:b + 3],
                               st5[:, b + 2:b + 3], op=ALU.mult))
        wait(V, vc.mark())
        vc.inc(V.tensor_tensor(st5[:, b + 4:b + 5], st5[:, b + 3:b + 4],
                               st5[:, b + 4:b + 5], op=ALU.subtract))
        wait(V, vc.mark())
        vc.inc(V.tensor_scalar_add(st5[:, b + 4:b + 5], st5[:, b + 4:b + 5], EPS))
        wait(V, vc.mark())
        vc.inc(V.reciprocal(st5[:, b + 5:b + 6], st5[:, b + 4:b + 5]))
        m_v5 = vc.mark()
        wait(Sc, m_v5)
        sc.inc(Sc.activation(st5[:, b + 6:b + 7], st5[:, b + 5:b + 6], AF.Sqrt))
        m_rstd5 = sc.mark()
        wait(V, m_rstd5, store5.get(t - 2))
        vc.inc(V.tensor_scalar(ot5[t % 2][:], zz, st5[:, b + 2:b + 3],
                               st5[:, b + 6:b + 7],
                               op0=ALU.subtract, op1=ALU.mult))
        app5[t] = vc.mark()
        store5[t] = qy.start(6 + t % 2, out[t * P:(t + 1) * P, :], ot5[t % 2][:],
                             pre=[app5[t]])
    wait(Y, *(c.mark() for c in qy.ctrs))
    wait(Y, *(c.mark() for c in qs.ctrs))
    wait(Y, *(c.mark() for c in qg.ctrs))
    nc.leave_named_scope("p5", scope_p5[0], False)
    p5.close()
    st.close()
    return nc


def _tables(pos):
    inv = 10000.0 ** (-np.arange(0, HD, 2, dtype=np.float64) / HD)
    fr = np.asarray(pos, dtype=np.float64)[:, None] * inv[None, :]

    def arr(a):
        return np.ascontiguousarray(
            a.astype(np.float32).reshape(-1, P, 64).transpose(1, 0, 2))
    return arr(np.cos(fr)), arr(np.sin(fr))


def prep_shared(Wq, Wk, Wv, Wo, Wgu, Wd):
    def r_img(W):
        W = np.asarray(W, np.float32)
        return np.ascontiguousarray(W.T.reshape(NC, P, W.shape[0]).transpose(1, 0, 2))

    wgu_t = np.ascontiguousarray(
        np.asarray(Wgu, np.float32).reshape(NG * 2, P, NC, P).transpose(0, 3, 2, 1))
    wd_t = np.ascontiguousarray(
        np.asarray(Wd, np.float32).reshape(NC, P, NG, P).transpose(0, 3, 2, 1))
    cos_kv, sin_kv = _tables(np.arange(S))
    return dict(
        wq_r=r_img(Wq), wk_r=r_img(Wk), wv_r=r_img(Wv), wo_r=r_img(Wo),
        wgu_t=wgu_t, wd_t=wd_t, cos_kv=cos_kv, sin_kv=sin_kv,
        ident=np.eye(P, dtype=np.float32),
        ones_bf=np.ones((P, P), dtype=ml_dtypes.bfloat16),
        ones_f32=np.ones((P, 1), dtype=np.float32),
    )


def prep_core(core, x, shared):
    b = core % 4
    blocks = BLOCKS_LO if core < 4 else BLOCKS_HI
    x_b = np.asarray(x[b], np.float32)
    x_own = np.ascontiguousarray(
        np.concatenate([x_b[512 * i:512 * (i + 1)] for i in blocks], axis=0))
    pos = np.concatenate([np.arange(512 * i, 512 * (i + 1)) for i in blocks])
    cos_own, sin_own = _tables(pos)
    m = np.zeros((P, 4, WIN, 512), dtype=np.float32)
    kk = np.arange(P)[:, None]
    qq = np.arange(512)[None, :]
    for s in range(4):
        for w in range(WIN):
            kpos = P * (NKT[s] - WIN + w) + kk
            m[:, s, w, :] = (kpos <= 512 * blocks[s] + qq).astype(np.float32)
    d = dict(shared)
    d.update(x_own=x_own, x_kv=np.ascontiguousarray(x_b),
             cos_own=cos_own, sin_own=sin_own,
             maskT=m.astype(ml_dtypes.bfloat16))
    return d, b, blocks


def kernel(L0_attn_hidden_states, L0_attn_Wq, L0_attn_Wk, L0_attn_Wv,
           L0_attn_Wo, L0_ffn_w_gate_up, L0_ffn_w_down):
    x = np.asarray(L0_attn_hidden_states, np.float32)
    shared = prep_shared(L0_attn_Wq, L0_attn_Wk, L0_attn_Wv, L0_attn_Wo,
                         L0_ffn_w_gate_up, L0_ffn_w_down)
    in_maps, metas = [], []
    for c in range(8):
        d, b, blocks = prep_core(c, x, shared)
        in_maps.append(d)
        metas.append((b, blocks))
    nc = build_nc()
    res = run_bass_kernel_spmd(nc, in_maps, core_ids=list(range(8)))
    outp = np.zeros((4, S, H), dtype=np.float32)
    for c in range(8):
        b, blocks = metas[c]
        o = res.results[c]["out"]
        for k, i in enumerate(blocks):
            outp[b, 512 * i:512 * (i + 1)] = o[512 * k:512 * (k + 1)]
    return outp
